# version 03 (final)
# baseline (speedup 1.0000x reference)
import numpy as np
import ml_dtypes
from contextlib import ExitStack

import concourse.bass as bass
import concourse.mybir as mybir
from concourse.bass_utils import run_bass_kernel_spmd

F32 = mybir.dt.float32
BF16 = mybir.dt.bfloat16
AF = mybir.ActivationFunctionType
ALU = mybir.AluOpType
AX = mybir.AxisListType

D = 4096
I = 11008
TOK = 1024
NT = 8
KC = 32
ANC = 4096
NAT = 32
KEEP = 5504.0
ROUNDS = 21
LN_EPS = 1e-5
SC_EPS = 1e-6
BIG = 1e30

STRIPS = [(j * 512, 512) for j in range(21)] + [(21 * 512, 256)]
HSTRIPS = [(j * 512, 512) for j in range(8)]
BCH = [(j * 1376, 1376) for j in range(8)]


def build_nc():
    nc = bass.Bass("TRN2", target_bir_lowering=False, debug=False)

    x_in = nc.dram_tensor("x", [TOK, D], F32, kind="ExternalInput")
    xa_in = nc.dram_tensor("xa", [ANC, D], F32, kind="ExternalInput")
    wg_in = nc.dram_tensor("wg", [I, D], F32, kind="ExternalInput")
    wu_in = nc.dram_tensor("wu", [I, D], F32, kind="ExternalInput")
    wd_in = nc.dram_tensor("wd", [D, I], F32, kind="ExternalInput")
    W_in = nc.dram_tensor("Wm", [D, D], F32, kind="ExternalInput")
    b_in = nc.dram_tensor("bv", [1, D], F32, kind="ExternalInput")
    A_in = nc.dram_tensor("Am", [D, 32], F32, kind="ExternalInput")
    Bm_in = nc.dram_tensor("Bm", [32, D], F32, kind="ExternalInput")
    Lm_in = nc.dram_tensor("Lm", [8, 32], F32, kind="ExternalInput")
    idm_in = nc.dram_tensor("idm", [128, 128], F32, kind="ExternalInput")
    o_out = nc.dram_tensor("o", [TOK, D], F32, kind="ExternalOutput")

    wg16 = nc.dram_tensor("wg16", [I, D], BF16)
    wu16 = nc.dram_tensor("wu16", [I, D], BF16)
    wd16 = nc.dram_tensor("wd16", [D, I], BF16)
    W16 = nc.dram_tensor("W16", [D, D], BF16)
    A16d = nc.dram_tensor("A16d", [D, 32], BF16)
    Bm16d = nc.dram_tensor("Bm16d", [32, D], BF16)
    b16d = nc.dram_tensor("b16d", [1, D], BF16)
    Lm16d = nc.dram_tensor("Lm16d", [8, 32], BF16)
    s_scr = nc.dram_tensor("s_scr", [TOK, I], F32)
    ms_scr = nc.dram_tensor("ms_scr", [TOK, I], BF16)
    st_scr = nc.dram_tensor("st_scr", [TOK, I], BF16)
    g_scr = nc.dram_tensor("g_scr", [TOK, D], BF16)

    ctx = ExitStack()
    e = ctx.enter_context

    sems = {k: e(nc.semaphore(k)) for k in ["qS", "qA", "qP", "PE", "DV", "AC"]}
    cnt = {k: 0 for k in sems}
    lastw = {}
    EK = {"SY": "SY", "AC": "AC", "DV": "DV", "PE": "PE", "GP": "GP"}

    def emit(key, instr, inc):
        instr.then_inc(sems[key], inc)
        cnt[key] += inc
        return cnt[key]

    def wt(eng, wkey, skey, val):
        if val is None or val <= 0:
            return
        if lastw.get((wkey, skey), -1) >= val:
            return
        eng.wait_ge(sems[skey], val)
        lastw[(wkey, skey)] = val

    ws = [e(nc.sbuf_tensor(f"ws{i}", [128, 512], BF16)) for i in range(4)]
    silu_st = [e(nc.sbuf_tensor(f"sst{i}", [128, 512], F32)) for i in range(2)]
    g_st = [e(nc.sbuf_tensor(f"gst{i}", [128, 512], BF16)) for i in range(2)]
    ms_st = [e(nc.sbuf_tensor(f"msst{i}", [128, 688], BF16)) for i in range(2)]
    sch = [e(nc.sbuf_tensor(f"sch{i}", [128, 688], F32)) for i in range(2)]
    dumpD = e(nc.sbuf_tensor("dumpD", [128, 1376], F32))
    dumpA = e(nc.sbuf_tensor("dumpA", [128, 1376], F32))
    lo_t = e(nc.sbuf_tensor("lo_t", [128, 8], F32))
    hi_t = e(nc.sbuf_tensor("hi_t", [128, 8], F32))
    cl_t = e(nc.sbuf_tensor("cl_t", [128, 8], F32))
    mid_t = e(nc.sbuf_tensor("mid_t", [128, 8], F32))
    thr_t = e(nc.sbuf_tensor("thr_t", [128, 8], F32))
    cellsD = e(nc.sbuf_tensor("cellsD", [128, 16], F32))
    cellsA = e(nc.sbuf_tensor("cellsA", [128, 16], F32))
    m8 = e(nc.sbuf_tensor("m8", [128, 160], F32))
    cand = e(nc.sbuf_tensor("cand", [128, 16], F32))
    tiny = e(nc.sbuf_tensor("tiny", [128, 16], F32))
    upd = e(nc.sbuf_tensor("upd", [128, 8], F32))
    zp16 = e(nc.sbuf_tensor("zp16", [32, TOK], BF16))
    b16c = e(nc.sbuf_tensor("b16c", [1, 512], BF16))
    Bm16c = e(nc.sbuf_tensor("Bm16c", [32, 512], BF16))
    ones1 = e(nc.sbuf_tensor("ones1", [1, 128], BF16))
    xn_fm = e(nc.sbuf_tensor("xn_fm", [128, KC * TOK], BF16))

    p8 = [e(nc.psum_tensor(f"p8_{i}", [128, 512], F32)) for i in range(8)]

    blk = e(nc.Block())

    sw = {}
    ps_free = {}
    ring_free = {}
    cast_marks = {}

    @blk.gpsimd
    def _(gp):
        emit("qP", gp.dma_start(out=b16d[:, :], in_=b_in[:, :]), 16)
        emit("qP", gp.dma_start(out=Lm16d[:, :], in_=Lm_in[:, :]), 16)
        cast_marks["b"] = cnt["qP"]
        emit("qP", gp.dma_start(out=A16d[:, :], in_=A_in[:, :]), 16)
        cast_marks["A"] = cnt["qP"]
        emit("qP", gp.dma_start(out=Bm16d[:, :], in_=Bm_in[:, :]), 16)
        cast_marks["Bm"] = cnt["qP"]
        for j in range(8):
            r0, r1 = j * 1376, (j + 1) * 1376
            emit("qP", gp.dma_start(out=wg16[r0:r1, :], in_=wg_in[r0:r1, :]), 16)
            cast_marks[("wg", j)] = cnt["qP"]
        for j in range(8):
            r0, r1 = j * 1376, (j + 1) * 1376
            emit("qP", gp.dma_start(out=wu16[r0:r1, :], in_=wu_in[r0:r1, :]), 16)
            cast_marks[("wu", j)] = cnt["qP"]
        for j in range(4):
            r0, r1 = j * 1024, (j + 1) * 1024
            emit("qP", gp.dma_start(out=W16[r0:r1, :], in_=W_in[r0:r1, :]), 16)
            cast_marks[("W", j)] = cnt["qP"]
        for j in range(4):
            r0, r1 = j * 1024, (j + 1) * 1024
            emit("qP", gp.dma_start(out=wd16[r0:r1, :], in_=wd_in[r0:r1, :]), 16)
            cast_marks[("wd", j)] = cnt["qP"]

    s1 = ExitStack()
    e1 = s1.enter_context
    xbuf = [e1(nc.sbuf_tensor(f"xbuf{i}", [128, D], F32)) for i in range(2)]
    axn = e1(nc.sbuf_tensor("axn", [128, D], BF16))
    idm = e1(nc.sbuf_tensor("idms", [128, 128], F32))
    A16 = e1(nc.sbuf_tensor("A16", [128, KC * 32], BF16))
    W8T = e1(nc.sbuf_tensor("W8T", [128, KC * 8], BF16))
    W8st = e1(nc.sbuf_tensor("W8st", [8, 128], F32))
    Lm16 = e1(nc.sbuf_tensor("Lm16", [8, 32], BF16))
    LmT16 = e1(nc.sbuf_tensor("LmT16", [32, 8], BF16))
    Lmst = e1(nc.sbuf_tensor("Lmst", [8, 32], F32))
    b8r = e1(nc.sbuf_tensor("b8r", [1, 8], BF16))
    zfm = e1(nc.sbuf_tensor("zfm", [32, TOK], F32))
    pmix = e1(nc.sbuf_tensor("pmix", [32, TOK], F32))
    za16 = e1(nc.sbuf_tensor("za16", [32, 128], BF16))
    scl2 = e1(nc.sbuf_tensor("scl2", [128, 2], F32))
    Hmax = e1(nc.sbuf_tensor("Hmax", [128, NAT], F32))
    dmax = e1(nc.sbuf_tensor("dmax", [128, NAT], F32))
    HsOwn = e1(nc.sbuf_tensor("HsOwn", [128, 32], F32))
    dsOwn = e1(nc.sbuf_tensor("dsOwn", [128, 32], F32))
    wgtO = e1(nc.sbuf_tensor("wgtO", [128, 32], F32))
    wfa = e1(nc.sbuf_tensor("wfa", [8, 512], BF16))
    wf16 = e1(nc.sbuf_tensor("wf16", [8, TOK], BF16))
    gt8 = [e1(nc.sbuf_tensor(f"gt8{i}", [128, 8], F32)) for i in range(4)]
    onesr = e1(nc.sbuf_tensor("onesr", [1, 128], F32))
    sclst = e1(nc.sbuf_tensor("sclst", [1, 2], F32))
    stats = e1(nc.sbuf_tensor("stats", [128, 16], F32))

    @blk.sync
    def _(sy):
        emit("qS", sy.dma_start(out=idm[:, :], in_=idm_in[:, :]), 16)
        emit("qS", sy.dma_start(out=Lmst[:, :], in_=Lm_in[:, :]), 16)

    @blk.gpsimd
    def _(gp):
        wt(gp, "GP", "qP", cast_marks["A"])
        for kc in range(KC):
            emit("qP", gp.dma_start(out=A16[:, kc * 32:(kc + 1) * 32],
                                    in_=A16d[kc * 128:(kc + 1) * 128, :]), 16)
        wt(gp, "GP", "qP", cast_marks["b"])
        emit("qP", gp.dma_start(out=Lm16[:, :], in_=Lm_in[:, :]), 16)
        emit("qP", gp.dma_start(out=b8r[:, :], in_=b16d[0:1, 0:8]), 16)
    small_ld = cnt["qP"]

    @blk.vector
    def _(dv):
        emit("DV", dv.memset(ones1[:, :], 1.0), 1)
        emit("DV", dv.memset(onesr[:, :], 1.0), 1)
        emit("DV", dv.memset(Hmax[:, :], 0.0), 1)
        emit("DV", dv.memset(dmax[:, :], 0.0), 1)

    @blk.tensor
    def _(pe):
        wt(pe, "PE", "qS", cnt["qS"])
        emit("PE", pe.transpose(p8[0][0:32, 0:8], Lmst[0:8, :], idm[0:8, 0:8]), 1)
        emit("PE", pe.drain(), 1)
    lmT_pe = cnt["PE"]

    @blk.scalar
    def _(ac):
        wt(ac, "AC", "PE", lmT_pe)
        emit("AC", ac.copy(out=LmT16[:, :], in_=p8[0][0:32, 0:8]), 1)
        emit("AC", ac.drain(), 1)
    lmT_done = cnt["AC"]

    for kc in range(KC):
        @blk.sync
        def _(sy, kc=kc):
            wt(sy, "SY", "AC", sw.get("w8_evac", 0))
            emit("qS", sy.dma_start(out=W8st[0:8, :], in_=W_in[0:8, kc * 128:(kc + 1) * 128]), 16)
        w8_ld = cnt["qS"]

        @blk.tensor
        def _(pe, kc=kc, w8_ld=w8_ld):
            wt(pe, "PE", "qS", w8_ld)
            wt(pe, "PE", "AC", sw.get("w8_evac", 0))
            emit("PE", pe.transpose(p8[1][:, 0:8], W8st[0:8, :], idm[0:8, 0:8]), 1)
            emit("PE", pe.drain(), 1)
        w8_pe = cnt["PE"]

        @blk.scalar
        def _(ac, kc=kc, w8_pe=w8_pe):
            wt(ac, "AC", "PE", w8_pe)
            emit("AC", ac.copy(out=W8T[:, kc * 8:(kc + 1) * 8], in_=p8[1][:, 0:8]), 1)
            emit("AC", ac.drain(), 1)
            sw["w8_evac"] = cnt["AC"]

    tiles = [("own", j) for j in range(NT)] + [("anc", j) for j in range(NAT)]
    xfree = [(None, 0), (None, 0)]

    for ti, (kind, idx) in enumerate(tiles):
        bj = ti % 2
        src = x_in if kind == "own" else xa_in
        row0 = idx * 128

        @blk.sync
        def _(sy, bj=bj, src=src, row0=row0):
            k, v = xfree[bj]
            if k:
                wt(sy, "SY", k, v)
            emit("qS", sy.dma_start(out=xbuf[bj][:, :], in_=src[row0:row0 + 128, :]), 16)
        x_ld = cnt["qS"]

        @blk.vector
        def _(dv, bj=bj, x_ld=x_ld):
            wt(dv, "DV", "qS", x_ld)
            emit("DV", dv.reduce_sum(out=stats[:, 0:1], in_=xbuf[bj][:, :], axis=AX.X), 1)

        @blk.scalar
        def _(ac, bj=bj, x_ld=x_ld):
            wt(ac, "AC", "qS", x_ld)
            for q, (c0, csz) in enumerate([(0, 1376), (1376, 1376), (2752, 1344)]):
                emit("AC", ac.activation(out=dumpA[:, 0:csz], in_=xbuf[bj][:, c0:c0 + csz],
                                         func=AF.Square, accum_out=stats[:, 4 + q:5 + q]), 1)
        sq_done = cnt["AC"]

        @blk.vector
        def _(dv, sq_done=sq_done):
            wt(dv, "DV", "AC", sq_done)
            wt(dv, "DV", "DV", cnt["DV"])
            emit("DV", dv.tensor_scalar(out=stats[:, 1:2], in0=stats[:, 0:1], scalar1=1.0 / D,
                                        scalar2=None, op0=ALU.mult), 1)
            emit("DV", dv.tensor_tensor(out=stats[:, 8:9], in0=stats[:, 4:5], in1=stats[:, 5:6], op=ALU.add), 1)
            wt(dv, "DV", "DV", cnt["DV"])
            emit("DV", dv.tensor_tensor(out=stats[:, 8:9], in0=stats[:, 8:9], in1=stats[:, 6:7], op=ALU.add), 1)
            wt(dv, "DV", "DV", cnt["DV"])
            emit("DV", dv.tensor_scalar(out=stats[:, 8:9], in0=stats[:, 8:9], scalar1=1.0 / D,
                                        scalar2=None, op0=ALU.mult), 1)
            emit("DV", dv.tensor_tensor(out=stats[:, 2:3], in0=stats[:, 1:2], in1=stats[:, 1:2], op=ALU.mult), 1)
            wt(dv, "DV", "DV", cnt["DV"])
            emit("DV", dv.tensor_tensor(out=stats[:, 3:4], in0=stats[:, 8:9], in1=stats[:, 2:3], op=ALU.subtract), 1)
            wt(dv, "DV", "DV", cnt["DV"])
            emit("DV", dv.tensor_scalar(out=stats[:, 3:4], in0=stats[:, 3:4], scalar1=LN_EPS,
                                        scalar2=None, op0=ALU.add), 1)
        var_done = cnt["DV"]

        @blk.scalar
        def _(ac, var_done=var_done):
            wt(ac, "AC", "DV", var_done)
            emit("AC", ac.activation(out=stats[:, 10:11], in_=stats[:, 3:4], func=AF.Sqrt), 1)
        sd_done = cnt["AC"]

        @blk.vector
        def _(dv, bj=bj, sd_done=sd_done):
            wt(dv, "DV", "AC", sd_done)
            emit("DV", dv.reciprocal(out=stats[:, 11:12], in_=stats[:, 10:11]), 1)
            wt(dv, "DV", "DV", cnt["DV"])
            emit("DV", dv.tensor_scalar(out=xbuf[bj][:, :], in0=xbuf[bj][:, :],
                                        scalar1=stats[:, 1:2], scalar2=stats[:, 11:12],
                                        op0=ALU.subtract, op1=ALU.mult), 1)
            emit("DV", dv.drain(), 1)
        norm_done = cnt["DV"]

        for kc in range(KC):
            pj = 2 + (kc % 6)

            @blk.tensor
            def _(pe, pj=pj, kc=kc, bj=bj, norm_done=norm_done):
                wt(pe, "PE", "DV", norm_done)
                k, v = sw.get(("tp_free", pj), (None, 0))
                if k:
                    wt(pe, "PE", k, v)
                emit("PE", pe.transpose(p8[pj][:, 0:128], xbuf[bj][:, kc * 128:(kc + 1) * 128], idm[:, :]), 1)
                emit("PE", pe.drain(), 1)
            tp_pe = cnt["PE"]

            @blk.vector
            def _(dv, pj=pj, kc=kc, kind=kind, idx=idx, tp_pe=tp_pe):
                wt(dv, "DV", "PE", tp_pe)
                if kind == "own":
                    out_ap = xn_fm[:, kc * TOK + idx * 128: kc * TOK + idx * 128 + 128]
                else:
                    out_ap = axn[:, kc * 128:(kc + 1) * 128]
                emit("DV", dv.tensor_copy(out=out_ap, in_=p8[pj][:, 0:128]), 1)
                sw[("tp_free", pj)] = ("DV", cnt["DV"])

        @blk.vector
        def _(dv):
            emit("DV", dv.drain(), 1)
        evac_done = cnt["DV"]
        xfree[bj] = ("PE", cnt["PE"])

        if kind == "anc":
            @blk.tensor
            def _(pe, evac_done=evac_done):
                wt(pe, "PE", "DV", evac_done)
                wt(pe, "PE", "AC", sw.get("w8_evac", 0))
                k, v = sw.get(("tp_free", "H"), (None, 0))
                if k:
                    wt(pe, "PE", k, v)
                wt(pe, "PE", "qP", small_ld)
                wt(pe, "PE", "AC", lmT_done)
                for kc in range(KC):
                    emit("PE", pe.matmul(p8[0][:, 0:8], axn[:, kc * 128:(kc + 1) * 128],
                                         W8T[:, kc * 8:(kc + 1) * 8], start=(kc == 0), stop=False), 1)
                emit("PE", pe.matmul(p8[0][:, 0:8], ones1[0:1, :], b8r[0:1, :], start=False, stop=True), 1)
                for kc in range(KC):
                    emit("PE", pe.matmul(p8[1][0:32, 0:128], A16[:, kc * 32:(kc + 1) * 32],
                                         axn[:, kc * 128:(kc + 1) * 128],
                                         start=(kc == 0), stop=(kc == KC - 1)), 1)
                emit("PE", pe.drain(), 1)
            hz_pe = cnt["PE"]

            @blk.scalar
            def _(ac, hz_pe=hz_pe):
                wt(ac, "AC", "PE", hz_pe)
                emit("AC", ac.copy(out=za16[:, :], in_=p8[1][0:32, 0:128]), 1)
                emit("AC", ac.drain(), 1)
            za_done = cnt["AC"]

            @blk.tensor
            def _(pe, za_done=za_done):
                wt(pe, "PE", "AC", za_done)
                emit("PE", pe.nop(cycle_cnt=2048), 1)
                emit("PE", pe.matmul(p8[1][:, 0:8], za16[:, :], LmT16[:, :], start=True, stop=True), 1)
                emit("PE", pe.drain(), 1)
            pd_pe = cnt["PE"]

            @blk.vector
            def _(dv, idx=idx, pd_pe=pd_pe):
                wt(dv, "DV", "PE", pd_pe)
                emit("DV", dv.nop(cycle_cnt=1024), 1)
                emit("DV", dv.reduce_max(out=Hmax[:, idx:idx + 1], in_=p8[0][:, 0:8],
                                         axis=AX.X, apply_absolute_value=True), 1)
                emit("DV", dv.reduce_max(out=dmax[:, idx:idx + 1], in_=p8[1][:, 0:8],
                                         axis=AX.X, apply_absolute_value=True), 1)
                if idx < 4:
                    emit("DV", dv.tensor_copy(out=HsOwn[:, idx * 8:(idx + 1) * 8], in_=p8[0][:, 0:8]), 1)
                    emit("DV", dv.tensor_copy(out=dsOwn[:, idx * 8:(idx + 1) * 8], in_=p8[1][:, 0:8]), 1)
                sw[("tp_free", "H")] = ("DV", cnt["DV"])

    @blk.tensor
    def _(pe):
        k, v = sw.get(("tp_free", "H"), (None, 0))
        if k:
            wt(pe, "PE", k, v)
        for half in range(2):
            for kc in range(KC):
                emit("PE", pe.matmul(p8[half][0:32, 0:512], A16[:, kc * 32:(kc + 1) * 32],
                                     xn_fm[:, kc * TOK + half * 512: kc * TOK + half * 512 + 512],
                                     start=(kc == 0), stop=(kc == KC - 1)), 1)
        emit("PE", pe.drain(), 1)
    z_pe = cnt["PE"]

    @blk.vector
    def _(dv, z_pe=z_pe):
        wt(dv, "DV", "PE", z_pe)
        emit("DV", dv.nop(cycle_cnt=1024), 1)
        for half in range(2):
            emit("DV", dv.tensor_copy(out=zfm[:, half * 512:(half + 1) * 512], in_=p8[half][0:32, 0:512]), 1)
        wt(dv, "DV", "DV", cnt["DV"])
        emit("DV", dv.reduce_max(out=tiny[:, 0:1], in_=Hmax[:, :], axis=AX.X), 1)
        emit("DV", dv.reduce_max(out=tiny[:, 1:2], in_=dmax[:, :], axis=AX.X), 1)
        emit("DV", dv.drain(), 1)
    colmax_done = cnt["DV"]

    @blk.tensor
    def _(pe, colmax_done=colmax_done):
        wt(pe, "PE", "DV", colmax_done)
        emit("PE", pe.transpose(p8[2][0:2, 0:128], tiny[:, 0:2], idm[:, :]), 1)
        emit("PE", pe.drain(), 1)
    t1_pe = cnt["PE"]

    @blk.vector
    def _(dv, t1_pe=t1_pe):
        wt(dv, "DV", "PE", t1_pe)
        emit("DV", dv.nop(cycle_cnt=1024), 1)
        emit("DV", dv.reduce_max(out=tiny[0:2, 2:3], in_=p8[2][0:2, 0:128], axis=AX.X), 1)
        wt(dv, "DV", "DV", cnt["DV"])
        emit("DV", dv.tensor_scalar(out=tiny[0:2, 3:4], in0=tiny[0:2, 2:3], scalar1=SC_EPS,
                                    scalar2=None, op0=ALU.max), 1)
        wt(dv, "DV", "DV", cnt["DV"])
        emit("DV", dv.reciprocal(out=tiny[0:2, 4:5], in_=tiny[0:2, 3:4]), 1)
        emit("DV", dv.drain(), 1)
    recip_done = cnt["DV"]

    @blk.tensor
    def _(pe, recip_done=recip_done):
        wt(pe, "PE", "DV", recip_done)
        emit("PE", pe.transpose(p8[3][0:1, 0:128], tiny[:, 4:5], idm[:, :]), 1)
        emit("PE", pe.drain(), 1)
    t2_pe = cnt["PE"]

    @blk.vector
    def _(dv, t2_pe=t2_pe):
        wt(dv, "DV", "PE", t2_pe)
        emit("DV", dv.nop(cycle_cnt=1024), 1)
        emit("DV", dv.tensor_copy(out=sclst[0:1, 0:2], in_=p8[3][0:1, 0:2]), 1)
        emit("DV", dv.drain(), 1)
    sclrow_done = cnt["DV"]

    @blk.tensor
    def _(pe, sclrow_done=sclrow_done):
        wt(pe, "PE", "DV", sclrow_done)
        emit("PE", pe.matmul(p8[3][:, 0:2], onesr[0:1, :], sclst[0:1, 0:2], start=True, stop=True), 1)
        emit("PE", pe.drain(), 1)
    bc_pe = cnt["PE"]

    @blk.vector
    def _(dv, bc_pe=bc_pe):
        wt(dv, "DV", "PE", bc_pe)
        emit("DV", dv.nop(cycle_cnt=1024), 1)
        emit("DV", dv.tensor_copy(out=scl2[:, :], in_=p8[3][:, 0:2]), 1)
    scl2_done = cnt["DV"]

    for j in range(4):
        @blk.vector
        def _(dv, j=j, scl2_done=scl2_done):
            wt(dv, "DV", "DV", max(scl2_done, cnt["DV"]))
            Hs = HsOwn[:, j * 8:(j + 1) * 8]
            ds = dsOwn[:, j * 8:(j + 1) * 8]
            emit("DV", dv.tensor_scalar(out=gt8[0][:, :], in0=Hs, scalar1=scl2[:, 0:1],
                                        scalar2=0.5, op0=ALU.mult, op1=ALU.mult), 1)
            emit("DV", dv.tensor_scalar(out=gt8[1][:, :], in0=ds, scalar1=scl2[:, 1:2],
                                        scalar2=0.5, op0=ALU.mult, op1=ALU.mult), 1)
            wt(dv, "DV", "DV", cnt["DV"])
            emit("DV", dv.tensor_tensor(out=gt8[0][:, :], in0=gt8[0][:, :], in1=gt8[1][:, :], op=ALU.add), 1)
            wt(dv, "DV", "DV", cnt["DV"])
            emit("DV", dv.reduce_max(out=tiny[:, 0:1], in_=gt8[0][:, :], axis=AX.X), 1)
            wt(dv, "DV", "DV", cnt["DV"])
            emit("DV", dv.tensor_scalar(out=tiny[:, 1:2], in0=tiny[:, 0:1], scalar1=-1.0,
                                        scalar2=None, op0=ALU.mult), 1)
        mx_done = cnt["DV"]

        @blk.scalar
        def _(ac, mx_done=mx_done):
            wt(ac, "AC", "DV", mx_done)
            emit("AC", ac.activation(out=gt8[1][:, :], in_=gt8[0][:, :], func=AF.Exp,
                                     bias=tiny[:, 1:2], accum_out=tiny[:, 2:3]), 1)
        ex_done = cnt["AC"]

        @blk.vector
        def _(dv, ex_done=ex_done):
            wt(dv, "DV", "AC", ex_done)
            emit("DV", dv.reciprocal(out=tiny[:, 3:4], in_=tiny[:, 2:3]), 1)
            wt(dv, "DV", "DV", cnt["DV"])
            emit("DV", dv.tensor_scalar(out=gt8[2][:, :], in0=gt8[1][:, :], scalar1=tiny[:, 3:4],
                                        scalar2=None, op0=ALU.mult), 1)
            wt(dv, "DV", "DV", cnt["DV"])
            emit("DV", dv.reduce_max(out=tiny[:, 4:5], in_=gt8[2][:, :], axis=AX.X), 1)
            wt(dv, "DV", "DV", cnt["DV"])
            emit("DV", dv.tensor_scalar(out=gt8[0][:, :], in0=gt8[2][:, :], scalar1=tiny[:, 4:5],
                                        scalar2=None, op0=ALU.is_ge), 1)
            wt(dv, "DV", "DV", cnt["DV"])
            emit("DV", dv.scalar_tensor_tensor(out=gt8[1][:, :], in0=gt8[0][:, :], scalar=-BIG,
                                               in1=gt8[2][:, :], op0=ALU.mult, op1=ALU.add), 1)
            wt(dv, "DV", "DV", cnt["DV"])
            emit("DV", dv.reduce_max(out=tiny[:, 5:6], in_=gt8[1][:, :], axis=AX.X), 1)
            wt(dv, "DV", "DV", cnt["DV"])
            emit("DV", dv.tensor_scalar(out=tiny[:, 6:7], in0=tiny[:, 5:6], scalar1=-2.0,
                                        scalar2=None, op0=ALU.mult), 1)
        v2_done = cnt["DV"]

        @blk.scalar
        def _(ac, v2_done=v2_done):
            wt(ac, "AC", "DV", v2_done)
            emit("AC", ac.activation(out=gt8[1][:, :], in_=gt8[2][:, :], func=AF.Sigmoid,
                                     bias=tiny[:, 6:7], scale=2.0), 1)
        sg_done = cnt["AC"]

        @blk.vector
        def _(dv, j=j, sg_done=sg_done):
            wt(dv, "DV", "AC", sg_done)
            emit("DV", dv.tensor_tensor(out=gt8[0][:, :], in0=gt8[2][:, :], in1=gt8[1][:, :], op=ALU.mult), 1)
            wt(dv, "DV", "DV", cnt["DV"])
            emit("DV", dv.reduce_sum(out=tiny[:, 7:8], in_=gt8[0][:, :], axis=AX.X), 1)
            wt(dv, "DV", "DV", cnt["DV"])
            emit("DV", dv.tensor_scalar(out=tiny[:, 8:9], in0=tiny[:, 7:8], scalar1=1e-9,
                                        scalar2=None, op0=ALU.add), 1)
            wt(dv, "DV", "DV", cnt["DV"])
            emit("DV", dv.reciprocal(out=tiny[:, 9:10], in_=tiny[:, 8:9]), 1)
            wt(dv, "DV", "DV", cnt["DV"])
            emit("DV", dv.tensor_scalar(out=wgtO[:, j * 8:(j + 1) * 8], in0=gt8[0][:, :],
                                        scalar1=tiny[:, 9:10], scalar2=None, op0=ALU.mult), 1)

    @blk.vector
    def _(dv):
        emit("DV", dv.drain(), 1)
    wgt_done = cnt["DV"]

    for j in range(4):
        @blk.tensor
        def _(pe, j=j, wgt_done=wgt_done):
            wt(pe, "PE", "DV", wgt_done)
            k, v = sw.get(("tp_free", "wf"), (None, 0))
            if k:
                wt(pe, "PE", k, v)
            emit("PE", pe.transpose(p8[4][0:8, 0:128], wgtO[:, j * 8:(j + 1) * 8], idm[:, :]), 1)
            emit("PE", pe.drain(), 1)
        wfT_pe = cnt["PE"]

        @blk.vector
        def _(dv, j=j, wfT_pe=wfT_pe):
            wt(dv, "DV", "PE", wfT_pe)
            emit("DV", dv.nop(cycle_cnt=1024), 1)
            emit("DV", dv.tensor_copy(out=wfa[:, j * 128:(j + 1) * 128], in_=p8[4][0:8, 0:128]), 1)
            sw[("tp_free", "wf")] = ("DV", cnt["DV"])

    @blk.vector
    def _(dv):
        wt(dv, "DV", "DV", cnt["DV"])
        wf2 = wf16[:, :].rearrange("e (a two) -> e a two", two=2)
        wfs = wfa[:, :].rearrange("e (a one) -> e a one", one=1)
        emit("DV", dv.tensor_copy(out=wf2[:, :, 0:1], in_=wfs), 1)
        emit("DV", dv.tensor_copy(out=wf2[:, :, 1:2], in_=wfs), 1)
        emit("DV", dv.drain(), 1)
    wf_done = cnt["DV"]

    @blk.tensor
    def _(pe, wf_done=wf_done):
        wt(pe, "PE", "DV", wf_done)
        for half in range(2):
            emit("PE", pe.matmul(p8[4 + half][0:32, 0:512], Lm16[:, :],
                                 wf16[:, half * 512:(half + 1) * 512], start=True, stop=True), 1)
        emit("PE", pe.drain(), 1)
    pm_pe = cnt["PE"]

    @blk.vector
    def _(dv, pm_pe=pm_pe):
        wt(dv, "DV", "PE", pm_pe)
        emit("DV", dv.nop(cycle_cnt=1024), 1)
        for half in range(2):
            emit("DV", dv.tensor_copy(out=pmix[:, half * 512:(half + 1) * 512], in_=p8[4 + half][0:32, 0:512]), 1)
        wt(dv, "DV", "DV", cnt["DV"])
        emit("DV", dv.tensor_tensor(out=pmix[:, :], in0=pmix[:, :], in1=zfm[:, :], op=ALU.mult), 1)
        emit("DV", dv.drain(), 1)
    zpm_done = cnt["DV"]

    @blk.scalar
    def _(ac, zpm_done=zpm_done):
        wt(ac, "AC", "DV", zpm_done)
        emit("AC", ac.activation(out=zp16[:, :], in_=pmix[:, :], func=AF.Copy, scale=0.5), 1)
        emit("AC", ac.drain(), 1)
    zp_done = cnt["AC"]

    sweep_pe = cnt["PE"]
    sweep_dv = cnt["DV"]
    sweep_ac = cnt["AC"]
    s1.close()

    s2 = ExitStack()
    e2 = s2.enter_context
    abuf = [e2(nc.sbuf_tensor(f"abuf{i}", [128, I], F32)) for i in range(2)]

    def phase_mlp(w16, castkey, half, evac, prefix, entry=None):
        tbs = list(range(half * 4, half * 4 + 4)) if half is not None else list(range(8))
        pslots = list(range(4)) if half is not None else list(range(8))
        nring = [0]
        if entry is not None:
            @blk.tensor
            def _(pe):
                entry(pe)
        for si, (off, sz) in enumerate(STRIPS):
            for kc in range(KC):
                slot = nring[0] % 4
                nring[0] += 1

                @blk.sync
                def _(sy, slot=slot, off=off, sz=sz, kc=kc):
                    wt(sy, "SY", "PE", ring_free.get((prefix, slot), 0))
                    wt(sy, "SY", "qP", cast_marks[(castkey, min(7, (off + sz - 1) // 1376))])
                    emit("qS", sy.dma_start(out=ws[slot][:, 0:sz],
                                            in_=w16[off:off + sz, kc * 128:(kc + 1) * 128],
                                            transpose=True), 16)
                ldm = cnt["qS"]

                @blk.tensor
                def _(pe, slot=slot, kc=kc, sz=sz, ldm=ldm):
                    wt(pe, "PE", "qS", ldm)
                    for j, tb in enumerate(tbs):
                        pj = pslots[j]
                        if kc == 0:
                            k, v = ps_free.get((prefix, pj), (None, 0))
                            if k:
                                wt(pe, "PE", k, v)
                        emit("PE", pe.matmul(p8[pj][:, 0:sz],
                                             xn_fm[:, kc * TOK + tb * 128: kc * TOK + tb * 128 + 128],
                                             ws[slot][:, 0:sz],
                                             start=(kc == 0), stop=(kc == KC - 1)), 1)
                    ring_free[(prefix, slot)] = cnt["PE"]

            @blk.tensor
            def _(pe):
                emit("PE", pe.drain(), 1)
            mm_done = cnt["PE"]

            for j, tb in enumerate(tbs):
                pj = pslots[j]
                if evac == "silu":
                    @blk.scalar
                    def _(ac, pj=pj, tb=tb, off=off, sz=sz, j=j, mm_done=mm_done):
                        wt(ac, "AC", "PE", mm_done)
                        wt(ac, "AC", "qA", sw.get(("sstq", j % 2), 0))
                        emit("AC", ac.activation(out=silu_st[j % 2][:, 0:sz], in_=p8[pj][:, 0:sz],
                                                 func=AF.Silu), 1)
                        ps_free[(prefix, pj)] = ("AC", cnt["AC"])
                        emit("AC", ac.drain(), 1)
                        ac.wait_ge(sems["AC"], cnt["AC"])
                        emit("qA", ac.dma_start(out=s_scr[tb * 128:(tb + 1) * 128, off:off + sz],
                                                in_=silu_st[j % 2][:, 0:sz]), 16)
                        sw[("sstq", j % 2)] = cnt["qA"]
                else:
                    @blk.sync
                    def _(sy, tb=tb, off=off, sz=sz, j=j):
                        wt(sy, "SY", "DV", sw.get(("mslf", j % 2), 0))
                        emit("qS", sy.dma_start(out=ms_st[j % 2][:, 0:sz],
                                                in_=ms_scr[tb * 128:(tb + 1) * 128, off:off + sz]), 16)
                    msq = cnt["qS"]

                    @blk.vector
                    def _(dv, pj=pj, sz=sz, j=j, mm_done=mm_done, msq=msq):
                        wt(dv, "DV", "PE", mm_done)
                        if j == 0:
                            emit("DV", dv.nop(cycle_cnt=1024), 1)
                        wt(dv, "DV", "qS", msq)
                        wt(dv, "DV", "qA", sw.get(("gstq", j % 2), 0))
                        emit("DV", dv.tensor_tensor(out=g_st[j % 2][:, 0:sz],
                                                    in0=ms_st[j % 2][:, 0:sz],
                                                    in1=p8[pj][:, 0:sz], op=ALU.mult), 1)
                        ps_free[(prefix, pj)] = ("DV", cnt["DV"])
                        emit("DV", dv.drain(), 1)
                        sw[("mslf", j % 2)] = cnt["DV"]
                    msf = cnt["DV"]

                    @blk.scalar
                    def _(ac, tb=tb, off=off, sz=sz, j=j, msf=msf):
                        wt(ac, "AC", "DV", msf)
                        emit("qA", ac.dma_start(out=st_scr[tb * 128:(tb + 1) * 128, off:off + sz],
                                                in_=g_st[j % 2][:, 0:sz]), 16)
                        sw[("gstq", j % 2)] = cnt["qA"]

    def a_entry(pe):
        wt(pe, "PE", "AC", zp_done)
        wt(pe, "PE", "DV", sweep_dv)
    phase_mlp(wg16, "wg", 0, "silu", "A1", entry=a_entry)
    for t in range(4):
        sw[("sdone", t)] = cnt["qA"]
    for j in range(4):
        ps_free[("A2", j)] = ps_free.get(("A1", j), (None, 0))
    phase_mlp(wg16, "wg", 1, "silu", "A2")
    for t in range(4, 8):
        sw[("sdone", t)] = cnt["qA"]
    a_done_ac = cnt["AC"]

    abuf_free = [(None, 0), (None, 0)]

    def b_pair(tD, tA):
        for t, bj in ((tD, 0), (tA, 1)):
            @blk.sync
            def _(sy, t=t, bj=bj):
                k, v = abuf_free[bj]
                if k:
                    wt(sy, "SY", k, v)
                wt(sy, "SY", "qA", sw.get(("sdone", t), 0))
                emit("qS", sy.dma_start(out=abuf[bj][:, :], in_=s_scr[t * 128:(t + 1) * 128, :]), 16)
            sw[("sload", t)] = cnt["qS"]

        @blk.scalar
        def _(ac):
            wt(ac, "AC", "qS", sw[("sload", tA)])
            wt(ac, "AC", "AC", cnt["AC"])
            emit("AC", ac.activation(out=abuf[0][:, :], in_=abuf[0][:, :], func=AF.Abs), 1)
            emit("AC", ac.activation(out=abuf[1][:, :], in_=abuf[1][:, :], func=AF.Abs), 1)
        abs_done = cnt["AC"]

        @blk.vector
        def _(dv, abs_done=abs_done):
            wt(dv, "DV", "AC", abs_done)
            emit("DV", dv.reduce_max(out=hi_t[:, tD:tD + 1], in_=abuf[0][:, :], axis=AX.X), 1)
            emit("DV", dv.reduce_max(out=hi_t[:, tA:tA + 1], in_=abuf[1][:, :], axis=AX.X), 1)
            emit("DV", dv.memset(mid_t[:, tD:tD + 2], 0.0), 1)
            emit("DV", dv.memset(lo_t[:, tD:tD + 2], 0.0), 1)
            emit("DV", dv.memset(cl_t[:, tD:tD + 2], float(I)), 1)
        mid_ready = cnt["DV"]

        for r in range(ROUNDS):
            @blk.vector
            def _(dv):
                wt(dv, "DV", "DV", cnt["DV"])
                for ci, (c0, csz) in enumerate(BCH):
                    emit("DV", dv.tensor_scalar(out=dumpD[:, 0:csz], in0=abuf[0][:, c0:c0 + csz],
                                                scalar1=mid_t[:, tD:tD + 1], scalar2=0.0,
                                                op0=ALU.is_gt, op1=ALU.add,
                                                accum_out=cellsD[:, ci:ci + 1]), 1)
            dcnt = cnt["DV"]

            @blk.scalar
            def _(ac, mid_ready=mid_ready):
                wt(ac, "AC", "DV", mid_ready)
                wt(ac, "AC", "AC", cnt["AC"])
                for ci, (c0, csz) in enumerate(BCH):
                    emit("AC", ac.activation(out=dumpA[:, 0:csz], in_=abuf[1][:, c0:c0 + csz],
                                             func=AF.Sign, bias=mid_t[:, tA:tA + 1], scale=-1.0,
                                             accum_out=cellsA[:, ci:ci + 1]), 1)
            acnt = cnt["AC"]

            @blk.vector
            def _(dv, dcnt=dcnt, acnt=acnt):
                S2 = slice(tD, tD + 2)
                wt(dv, "DV", "DV", dcnt)
                emit("DV", dv.reduce_sum(out=upd[:, 0:1], in_=cellsD[:, 0:8], axis=AX.X), 1)
                wt(dv, "DV", "AC", acnt)
                emit("DV", dv.reduce_sum(out=upd[:, 2:3], in_=cellsA[:, 0:8], axis=AX.X), 1)
                wt(dv, "DV", "DV", cnt["DV"])
                emit("DV", dv.tensor_scalar(out=upd[:, 1:2], in0=upd[:, 2:3], scalar1=-0.5,
                                            scalar2=KEEP, op0=ALU.mult, op1=ALU.add), 1)
                wt(dv, "DV", "DV", cnt["DV"])
                cG = upd[:, 0:2]
                emit("DV", dv.tensor_scalar(out=upd[:, 2:4], in0=cG, scalar1=KEEP - 0.25,
                                            scalar2=None, op0=ALU.is_gt), 1)
                wt(dv, "DV", "DV", cnt["DV"])
                sel = upd[:, 2:4]
                emit("DV", dv.tensor_scalar(out=upd[:, 4:6], in0=sel, scalar1=-1.0,
                                            scalar2=1.0, op0=ALU.mult, op1=ALU.add), 1)
                inv = upd[:, 4:6]
                emit("DV", dv.tensor_tensor(out=cellsD[:, 8:10], in0=mid_t[:, S2], in1=lo_t[:, S2], op=ALU.subtract), 1)
                wt(dv, "DV", "DV", cnt["DV"])
                emit("DV", dv.tensor_tensor(out=cellsD[:, 10:12], in0=sel, in1=cellsD[:, 8:10], op=ALU.mult), 1)
                wt(dv, "DV", "DV", cnt["DV"])
                emit("DV", dv.tensor_tensor(out=lo_t[:, S2], in0=lo_t[:, S2], in1=cellsD[:, 10:12], op=ALU.add), 1)
                emit("DV", dv.tensor_tensor(out=cellsD[:, 12:14], in0=mid_t[:, S2], in1=hi_t[:, S2], op=ALU.subtract), 1)
                wt(dv, "DV", "DV", cnt["DV"])
                emit("DV", dv.tensor_tensor(out=cellsD[:, 14:16], in0=inv, in1=cellsD[:, 12:14], op=ALU.mult), 1)
                wt(dv, "DV", "DV", cnt["DV"])
                emit("DV", dv.tensor_tensor(out=hi_t[:, S2], in0=hi_t[:, S2], in1=cellsD[:, 14:16], op=ALU.add), 1)
                emit("DV", dv.tensor_tensor(out=cellsA[:, 8:10], in0=cG, in1=cl_t[:, S2], op=ALU.subtract), 1)
                wt(dv, "DV", "DV", cnt["DV"])
                emit("DV", dv.tensor_tensor(out=cellsA[:, 10:12], in0=sel, in1=cellsA[:, 8:10], op=ALU.mult), 1)
                wt(dv, "DV", "DV", cnt["DV"])
                emit("DV", dv.tensor_tensor(out=cl_t[:, S2], in0=cl_t[:, S2], in1=cellsA[:, 10:12], op=ALU.add), 1)
                wt(dv, "DV", "DV", cnt["DV"])
                emit("DV", dv.tensor_tensor(out=cellsA[:, 12:14], in0=lo_t[:, S2], in1=hi_t[:, S2], op=ALU.add), 1)
                wt(dv, "DV", "DV", cnt["DV"])
                emit("DV", dv.tensor_scalar(out=mid_t[:, S2], in0=cellsA[:, 12:14], scalar1=0.5,
                                            scalar2=None, op0=ALU.mult), 1)
            mid_ready = cnt["DV"]

        @blk.vector
        def _(dv):
            wt(dv, "DV", "DV", cnt["DV"])
            for ci, (c0, csz) in enumerate(BCH):
                emit("DV", dv.tensor_scalar(out=dumpD[:, 0:csz], in0=abuf[1][:, c0:c0 + csz],
                                            scalar1=lo_t[:, tA:tA + 1], scalar2=0.0,
                                            op0=ALU.is_gt, op1=ALU.add,
                                            accum_out=cellsD[:, ci:ci + 1]), 1)
            wt(dv, "DV", "DV", cnt["DV"])
            emit("DV", dv.reduce_sum(out=cl_t[:, tA:tA + 1], in_=cellsD[:, 0:8], axis=AX.X), 1)

        for t, bj in ((tD, 0), (tA, 1)):
            @blk.vector
            def _(dv, t=t, bj=bj):
                wt(dv, "DV", "DV", cnt["DV"])
                for ci in range(16):
                    c0 = ci * 688
                    emit("DV", dv.tensor_scalar(out=dumpD[:, 0:688], in0=abuf[bj][:, c0:c0 + 688],
                                                scalar1=lo_t[:, t:t + 1], scalar2=-BIG,
                                                op0=ALU.is_le, op1=ALU.mult), 1)
                    wt(dv, "DV", "DV", cnt["DV"])
                    emit("DV", dv.tensor_tensor(out=dumpD[:, 0:688], in0=dumpD[:, 0:688],
                                                in1=abuf[bj][:, c0:c0 + 688], op=ALU.subtract), 1)
                    wt(dv, "DV", "DV", cnt["DV"])
                    emit("DV", dv.max(out=m8[:, ci * 8:(ci + 1) * 8], in_=dumpD[:, 0:688]), 1)
                    wt(dv, "DV", "DV", cnt["DV"])
                emit("DV", dv.max(out=cand[:, 0:8], in_=m8[:, 0:128]), 1)
                wt(dv, "DV", "DV", cnt["DV"])
                emit("DV", dv.match_replace(out=m8[:, 0:128], in_to_replace=cand[:, 0:8],
                                            in_values=m8[:, 0:128], imm_value=-BIG), 1)
                wt(dv, "DV", "DV", cnt["DV"])
                emit("DV", dv.max(out=cand[:, 8:16], in_=m8[:, 0:128]), 1)
                wt(dv, "DV", "DV", cnt["DV"])
                emit("DV", dv.tensor_scalar(out=tiny[:, 0:1], in0=cl_t[:, t:t + 1], scalar1=-KEEP,
                                            scalar2=None, op0=ALU.add), 1)
                wt(dv, "DV", "DV", cnt["DV"])
                for j8 in range(16):
                    emit("DV", dv.tensor_scalar(out=cellsD[:, j8:j8 + 1], in0=tiny[:, 0:1],
                                                scalar1=float(j8) + 0.5, scalar2=None, op0=ALU.is_gt), 1)
                    emit("DV", dv.tensor_scalar(out=cellsA[:, j8:j8 + 1], in0=tiny[:, 0:1],
                                                scalar1=float(j8) + 1.5, scalar2=None, op0=ALU.is_gt), 1)
                wt(dv, "DV", "DV", cnt["DV"])
                emit("DV", dv.tensor_tensor(out=m8[:, 128:144], in0=cellsD[:, 0:16], in1=cellsA[:, 0:16], op=ALU.subtract), 1)
                wt(dv, "DV", "DV", cnt["DV"])
                emit("DV", dv.tensor_tensor(out=m8[:, 128:144], in0=m8[:, 128:144], in1=cand[:, 0:16], op=ALU.mult), 1)
                wt(dv, "DV", "DV", cnt["DV"])
                emit("DV", dv.reduce_sum(out=tiny[:, 1:2], in_=m8[:, 128:144], axis=AX.X, negate=True), 1)
                wt(dv, "DV", "DV", cnt["DV"])
                emit("DV", dv.tensor_scalar(out=tiny[:, 2:3], in0=tiny[:, 0:1], scalar1=0.5,
                                            scalar2=None, op0=ALU.is_lt), 1)
                wt(dv, "DV", "DV", cnt["DV"])
                emit("DV", dv.tensor_tensor(out=tiny[:, 3:4], in0=tiny[:, 1:2], in1=lo_t[:, t:t + 1], op=ALU.subtract), 1)
                wt(dv, "DV", "DV", cnt["DV"])
                emit("DV", dv.tensor_tensor(out=tiny[:, 4:5], in0=tiny[:, 2:3], in1=tiny[:, 3:4], op=ALU.mult), 1)
                wt(dv, "DV", "DV", cnt["DV"])
                emit("DV", dv.tensor_tensor(out=thr_t[:, t:t + 1], in0=tiny[:, 1:2], in1=tiny[:, 4:5], op=ALU.subtract), 1)
            thr_done = cnt["DV"]

            for ci in range(16):
                c0 = ci * 688

                @blk.sync
                def _(sy, t=t, c0=c0, ci=ci):
                    wt(sy, "SY", "DV", sw.get(("schf", ci % 2), 0))
                    emit("qS", sy.dma_start(out=sch[ci % 2][:, :], in_=s_scr[t * 128:(t + 1) * 128, c0:c0 + 688]), 16)
                schq = cnt["qS"]

                @blk.vector
                def _(dv, t=t, bj=bj, c0=c0, ci=ci, thr_done=thr_done, schq=schq):
                    wt(dv, "DV", "DV", thr_done)
                    wt(dv, "DV", "qS", schq)
                    wt(dv, "DV", "qA", sw.get(("msstq", ci % 2), 0))
                    emit("DV", dv.scalar_tensor_tensor(out=ms_st[ci % 2][:, :],
                                                       in0=abuf[bj][:, c0:c0 + 688],
                                                       scalar=thr_t[:, t:t + 1],
                                                       in1=sch[ci % 2][:, :],
                                                       op0=ALU.is_gt, op1=ALU.mult), 1)
                    emit("DV", dv.drain(), 1)
                    sw[("schf", ci % 2)] = cnt["DV"]
                msf = cnt["DV"]

                @blk.scalar
                def _(ac, t=t, c0=c0, ci=ci, msf=msf):
                    wt(ac, "AC", "DV", msf)
                    emit("qA", ac.dma_start(out=ms_scr[t * 128:(t + 1) * 128, c0:c0 + 688],
                                            in_=ms_st[ci % 2][:, :]), 16)
                    sw[("msstq", ci % 2)] = cnt["qA"]

            abuf_free[bj] = ("DV", cnt["DV"])

    for pair in range(4):
        b_pair(2 * pair, 2 * pair + 1)
    b_dv = cnt["DV"]
    b_qa = cnt["qA"]

    @blk.tensor
    def _(pe):
        wt(pe, "PE", "AC", a_done_ac)
        wt(pe, "PE", "DV", b_dv)

    for hs_, (hoff, hsz) in enumerate(HSTRIPS):
        @blk.gpsimd
        def _(gp, hoff=hoff, hsz=hsz):
            wt(gp, "GP", "PE", sw.get("bm_pe", 0))
            wt(gp, "GP", "qP", cast_marks["Bm"])
            emit("qP", gp.dma_start(out=Bm16c[:, 0:hsz], in_=Bm16d[:, hoff:hoff + hsz]), 16)
            emit("qP", gp.dma_start(out=b16c[:, 0:hsz], in_=b16d[0:1, hoff:hoff + hsz]), 16)
        bm_ld = cnt["qP"]

        for half in range(2):
            tbs = list(range(half * 4, half * 4 + 4))
            for kc in range(KC):
                slot = sw.get("e_ring", 0) % 4
                sw["e_ring"] = sw.get("e_ring", 0) + 1

                @blk.sync
                def _(sy, slot=slot, kc=kc, hoff=hoff, hsz=hsz):
                    wt(sy, "SY", "PE", ring_free.get(("E", slot), 0))
                    wt(sy, "SY", "qP", cast_marks[("W", min(3, (hoff + hsz - 1) // 1024))])
                    emit("qS", sy.dma_start(out=ws[slot][:, 0:hsz],
                                            in_=W16[hoff:hoff + hsz, kc * 128:(kc + 1) * 128],
                                            transpose=True), 16)
                ldm = cnt["qS"]

                @blk.tensor
                def _(pe, slot=slot, kc=kc, tbs=tbs, hsz=hsz, ldm=ldm):
                    wt(pe, "PE", "qS", ldm)
                    for j, tb in enumerate(tbs):
                        if kc == 0:
                            k, v = ps_free.get(("E", j), (None, 0))
                            if k:
                                wt(pe, "PE", k, v)
                        emit("PE", pe.matmul(p8[j][:, 0:hsz],
                                             xn_fm[:, kc * TOK + tb * 128: kc * TOK + tb * 128 + 128],
                                             ws[slot][:, 0:hsz],
                                             start=(kc == 0), stop=False), 1)
                    ring_free[("E", slot)] = cnt["PE"]

            @blk.tensor
            def _(pe, tbs=tbs, hsz=hsz, bm_ld=bm_ld):
                wt(pe, "PE", "qP", bm_ld)
                wt(pe, "PE", "AC", zp_done)
                for j, tb in enumerate(tbs):
                    emit("PE", pe.matmul(p8[j][:, 0:hsz], zp16[:, tb * 128:(tb + 1) * 128],
                                         Bm16c[:, 0:hsz], start=False, stop=False), 1)
                    emit("PE", pe.matmul(p8[j][:, 0:hsz], ones1[0:1, :], b16c[0:1, 0:hsz],
                                         start=False, stop=True), 1)
                emit("PE", pe.drain(), 1)
            mmE = cnt["PE"]

            for j, tb in enumerate(tbs):
                @blk.scalar
                def _(ac, j=j, tb=tb, hoff=hoff, hsz=hsz, mmE=mmE):
                    wt(ac, "AC", "PE", mmE)
                    wt(ac, "AC", "qA", sw.get(("gstq", j % 2), 0))
                    emit("AC", ac.activation(out=g_st[j % 2][:, 0:hsz], in_=p8[j][:, 0:hsz], func=AF.Silu), 1)
                    ps_free[("E", j)] = ("AC", cnt["AC"])
                    emit("AC", ac.drain(), 1)
                    ac.wait_ge(sems["AC"], cnt["AC"])
                    emit("qA", ac.dma_start(out=g_scr[tb * 128:(tb + 1) * 128, hoff:hoff + hsz],
                                            in_=g_st[j % 2][:, 0:hsz]), 16)
                    sw[("gstq", j % 2)] = cnt["qA"]
        sw["bm_pe"] = cnt["PE"]
    e_qa = cnt["qA"]
    e_ac = cnt["AC"]

    def c_entry(pe):
        wt(pe, "PE", "DV", b_dv)
        wt(pe, "PE", "AC", e_ac)
        wt(pe, "PE", "qA", b_qa)
    for j in range(8):
        ps_free[("C", j)] = (None, 0)
    phase_mlp(wu16, "wu", None, "mask", "C", entry=c_entry)
    c_dv = cnt["DV"]
    c_qa = cnt["qA"]

    s2.close()

    s3 = ExitStack()
    e3 = s3.enter_context
    sfm = e3(nc.sbuf_tensor("sfm", [128, 86 * 512], BF16))
    gch = [e3(nc.sbuf_tensor(f"gch{i}", [128, 512], BF16)) for i in range(2)]
    xch = [e3(nc.sbuf_tensor(f"xch{i}", [128, 512], F32)) for i in range(2)]
    ost = [e3(nc.sbuf_tensor(f"ost{i}", [128, 512], F32)) for i in range(2)]

    for tbh in range(2):
        @blk.sync
        def _(sy, tbh=tbh):
            wt(sy, "SY", "qA", c_qa)
            wt(sy, "SY", "PE", sw.get("sfm_free", 0))
            for i2 in range(86):
                emit("qS", sy.dma_start(out=sfm[:, i2 * 512:(i2 + 1) * 512],
                                        in_=st_scr[tbh * 512:(tbh + 1) * 512, i2 * 128:(i2 + 1) * 128],
                                        transpose=True), 16)
        sfm_ld = cnt["qS"]

        for hs_, (hoff, hsz) in enumerate(HSTRIPS):
            for i2 in range(86):
                slot = sw.get("d_ring", 0) % 4
                sw["d_ring"] = sw.get("d_ring", 0) + 1

                @blk.sync
                def _(sy, slot=slot, i2=i2, hoff=hoff, hsz=hsz):
                    wt(sy, "SY", "PE", ring_free.get(("D", slot), 0))
                    wt(sy, "SY", "qP", cast_marks[("wd", min(3, (hoff + hsz - 1) // 1024))])
                    emit("qS", sy.dma_start(out=ws[slot][:, 0:hsz],
                                            in_=wd16[hoff:hoff + hsz, i2 * 128:(i2 + 1) * 128],
                                            transpose=True), 16)
                ldm = cnt["qS"]

                @blk.tensor
                def _(pe, slot=slot, i2=i2, hsz=hsz, ldm=ldm, sfm_ld=sfm_ld, tbh=tbh):
                    wt(pe, "PE", "qS", ldm)
                    wt(pe, "PE", "qS", sfm_ld)
                    if tbh == 0 and i2 == 0:
                        wt(pe, "PE", "DV", c_dv)
                    for j in range(4):
                        if i2 == 0:
                            k, v = ps_free.get(("D", j), (None, 0))
                            if k:
                                wt(pe, "PE", k, v)
                        emit("PE", pe.matmul(p8[j][:, 0:hsz],
                                             sfm[:, i2 * 512 + j * 128: i2 * 512 + j * 128 + 128],
                                             ws[slot][:, 0:hsz],
                                             start=(i2 == 0), stop=(i2 == 85)), 1)
                    ring_free[("D", slot)] = cnt["PE"]

            @blk.tensor
            def _(pe):
                emit("PE", pe.drain(), 1)
            mmD = cnt["PE"]

            for j in range(4):
                tb = tbh * 4 + j

                @blk.gpsimd
                def _(gp, tb=tb, hoff=hoff, hsz=hsz, j=j):
                    wt(gp, "GP", "DV", sw.get(("gxf", j % 2), 0))
                    wt(gp, "GP", "qA", e_qa)
                    emit("qP", gp.dma_start(out=gch[j % 2][:, 0:hsz],
                                            in_=g_scr[tb * 128:(tb + 1) * 128, hoff:hoff + hsz]), 16)
                    emit("qP", gp.dma_start(out=xch[j % 2][:, 0:hsz],
                                            in_=x_in[tb * 128:(tb + 1) * 128, hoff:hoff + hsz]), 16)
                gx_ld = cnt["qP"]

                @blk.vector
                def _(dv, j=j, hsz=hsz, mmD=mmD, gx_ld=gx_ld):
                    wt(dv, "DV", "PE", mmD)
                    if j == 0:
                        emit("DV", dv.nop(cycle_cnt=1024), 1)
                    wt(dv, "DV", "qP", gx_ld)
                    wt(dv, "DV", "qA", sw.get(("ostq", j % 2), 0))
                    emit("DV", dv.tensor_tensor(out=ost[j % 2][:, 0:hsz], in0=gch[j % 2][:, 0:hsz],
                                                in1=p8[j][:, 0:hsz], op=ALU.mult), 1)
                    wt(dv, "DV", "DV", cnt["DV"])
                    emit("DV", dv.tensor_tensor(out=ost[j % 2][:, 0:hsz], in0=ost[j % 2][:, 0:hsz],
                                                in1=xch[j % 2][:, 0:hsz], op=ALU.add), 1)
                    ps_free[("D", j)] = ("DV", cnt["DV"])
                    emit("DV", dv.drain(), 1)
                    sw[("gxf", j % 2)] = cnt["DV"]
                ev_done = cnt["DV"]

                @blk.scalar
                def _(ac, tb=tb, hoff=hoff, hsz=hsz, j=j, ev_done=ev_done):
                    wt(ac, "AC", "DV", ev_done)
                    emit("qA", ac.dma_start(out=o_out[tb * 128:(tb + 1) * 128, hoff:hoff + hsz],
                                            in_=ost[j % 2][:, 0:hsz]), 16)
                    sw[("ostq", j % 2)] = cnt["qA"]
        sw["sfm_free"] = cnt["PE"]

    @blk.sync
    def _(sy):
        sy.wait_ge(sems["qA"], cnt["qA"])
        sy.wait_ge(sems["qP"], cnt["qP"])
        sy.wait_ge(sems["qS"], cnt["qS"])

    s3.close()
    ctx.close()
    return nc


def kernel(L0_attn_x, L0_attn_w_gate, L0_attn_w_up, L0_attn_w_down,
           L0_ffn_W, L0_ffn_b, L0_ffn_A, L0_ffn_Bmat, L0_ffn_LiMEs):
    X = np.ascontiguousarray(np.asarray(L0_attn_x, dtype=np.float32))
    wg = np.ascontiguousarray(np.asarray(L0_attn_w_gate, dtype=np.float32))
    wu = np.ascontiguousarray(np.asarray(L0_attn_w_up, dtype=np.float32))
    wd = np.ascontiguousarray(np.asarray(L0_attn_w_down, dtype=np.float32))
    Wm = np.ascontiguousarray(np.asarray(L0_ffn_W, dtype=np.float32))
    bv = np.ascontiguousarray(np.asarray(L0_ffn_b, dtype=np.float32)).reshape(1, D)
    Am = np.ascontiguousarray(np.asarray(L0_ffn_A, dtype=np.float32))
    Bm = np.ascontiguousarray(np.asarray(L0_ffn_Bmat, dtype=np.float32))
    Lm = np.ascontiguousarray(np.asarray(L0_ffn_LiMEs, dtype=np.float32))

    anchors = np.ascontiguousarray(X[:, 1::2, :])
    idm = np.eye(128, dtype=np.float32)

    nc = build_nc()
    in_maps = []
    for c in range(8):
        bb, q = c // 4, c % 4
        own = np.ascontiguousarray(X[bb, q * TOK:(q + 1) * TOK, :])
        own_anc = anchors[bb, q * 512:(q + 1) * 512, :]
        rest = [anchors[b2][j * 512:(j + 1) * 512] for b2 in range(2) for j in range(4)
                if not (b2 == bb and j == q)]
        xa = np.ascontiguousarray(np.concatenate([own_anc] + rest, axis=0))
        in_maps.append({"x": own, "xa": xa, "wg": wg, "wu": wu, "wd": wd,
                        "Wm": Wm, "bv": bv, "Am": Am, "Bm": Bm, "Lm": Lm, "idm": idm})
    res = run_bass_kernel_spmd(nc, in_maps, core_ids=list(range(8)))
    out = np.empty((2, 4096, D), dtype=np.float32)
    for c in range(8):
        bb, q = c // 4, c % 4
        out[bb, q * TOK:(q + 1) * TOK, :] = res.results[c]["o"]
    return out
